# version 11
# baseline (speedup 1.0000x reference)
import os
import numpy as np
from contextlib import ExitStack

import concourse.bass as bass
import concourse.mybir as mybir
from concourse.bass_utils import run_bass_kernel_spmd

F32 = mybir.dt.float32
BF16 = mybir.dt.bfloat16
AF = mybir.ActivationFunctionType
ALU = mybir.AluOpType
AX = mybir.AxisListType

NCORES = 8
D = 2048
T = 1024
DC = 16
E = 16
FC = 6
NT = 2


def build_nc(phases=99, kperf=0):
    nc = bass.Bass("TRN2", target_bir_lowering=False,
                   detect_race_conditions=False)

    x_d = nc.dram_tensor("x", [D, T], F32, kind="ExternalInput")
    wzn_d = nc.dram_tensor("wzn", [16, 128, 2, 16, 128], F32, kind="ExternalInput")
    gu_d = nc.dram_tensor("gu", [16, 6, 128, 2, 16, 128], F32, kind="ExternalInput")
    down_d = nc.dram_tensor("down", [16, 16, 128, 6, 128], F32, kind="ExternalInput")
    rtr_d = nc.dram_tensor("rtr", [128, 256], F32, kind="ExternalInput")
    bz_d = nc.dram_tensor("bz", [128, 16], F32, kind="ExternalInput")
    bn_d = nc.dram_tensor("bn", [128, 16], F32, kind="ExternalInput")
    csel_d = nc.dram_tensor("csel", [128, 8], F32, kind="ExternalInput")
    idf_d = nc.dram_tensor("idf", [128, 128], F32, kind="ExternalInput")
    y_d = nc.dram_tensor("y", [D, T], F32, kind="ExternalOutput")

    h_scr = nc.dram_tensor("h_scr", [D, T], F32)
    dbg = os.environ.get("KDBG") == "1"
    if dbg:
        dbg_h = nc.dram_tensor("dbg_h", [D, T], F32, kind="ExternalOutput")
        dbg_o = nc.dram_tensor("dbg_o", [D, T], F32, kind="ExternalOutput")
        dbg_xn = nc.dram_tensor("dbg_xn", [D, T], F32, kind="ExternalOutput")
        dbg_r = nc.dram_tensor("dbg_r", [128, 512], F32, kind="ExternalOutput")
    cc_in = nc.dram_tensor("cc_in", [128, 32], F32)
    cc_out = nc.dram_tensor("cc_out", [NCORES * 128, 32], F32)

    ctx = ExitStack()
    sbl = lambda name, shape, dt: ctx.enter_context(
        nc.sbuf_tensor(name, shape, dt))

    xn = sbl("xn", [128, DC * T], BF16)
    bz_sb = sbl("bz_sb", [128, 16], F32)
    bn_sb = sbl("bn_sb", [128, 16], F32)
    negbz = sbl("negbz", [128, 16], F32)
    bnp05 = sbl("bnp05", [128, 16], F32)
    rtr_sb = sbl("rtr_sb", [128, 256], BF16)
    csel_sb = sbl("csel_sb", [128, 8], F32)
    idf_sb = sbl("idf_sb", [128, 128], F32)
    identb = sbl("identb", [128, 128], BF16)
    ones_bf = sbl("ones_bf", [128, 1], BF16)
    onesrow = sbl("onesrow", [1, 128], F32)
    s_out = sbl("s_out", [128, 32], F32)
    ag_sb = sbl("ag_sb", [128, NCORES * 32], F32)
    Hcur = sbl("Hcur", [128, 16], F32)
    H0 = sbl("H0", [128, 16], F32)
    tmp16a = sbl("tmp16a", [128, 16], F32)
    tmp16b = sbl("tmp16b", [128, 16], F32)
    c_tm = sbl("c_tm", [128, 128], F32)
    c_tmb = sbl("c_tmb", [128, 128], BF16)
    st_tm = sbl("st_tm", [128, 128], F32)
    e_all = sbl("e_all", [128, 128], F32)
    cmp1b = sbl("cmp1b", [128, 128], F32)
    cmp2b = sbl("cmp2b", [128, 128], F32)
    maskb = sbl("maskb", [128, 128], F32)
    rcols = sbl("rcols", [128, 64], F32)

    r1 = ExitStack()
    wzn_buf = [r1.enter_context(
        nc.sbuf_tensor(f"wznb{i}", [128, 4096], BF16, side="right"))
        for i in range(2)]

    sm_x = ctx.enter_context(nc.semaphore("sm_x"))
    sm_w = ctx.enter_context(nc.semaphore("sm_w"))
    sm_pe = ctx.enter_context(nc.semaphore("sm_pe"))
    sm_act = ctx.enter_context(nc.semaphore("sm_act"))
    sm_dve = ctx.enter_context(nc.semaphore("sm_dve"))
    sm_cc = ctx.enter_context(nc.semaphore("sm_cc"))
    sm_sd = ctx.enter_context(nc.semaphore("sm_sd"))
    csd = [0]

    cx = [0]; cw = [0]; cpe = [0]; cact = [0]; cdve = [0]

    def bump(c, n=1):
        c[0] += n
        return c[0]

    ev = {}

    p0 = ExitStack()
    s0 = lambda name, shape, dt: p0.enter_context(nc.sbuf_tensor(name, shape, dt))
    xs = s0("xs", [128, DC * T], F32)
    xbr = [s0(f"xbr{i}", [128, T], BF16) for i in range(2)]
    x2r = [s0(f"x2r{i}", [128, T], BF16) for i in range(2)]
    a_bc = s0("a_bc", [128, T], F32)
    b_bc = s0("b_bc", [128, T], F32)
    tmpT = s0("tmpT", [128, T], F32)
    rows = s0("rows", [1, 4096], F32)
    epscol = s0("epscol", [1, 1], F32)
    mrow = rows[0:1, 0:1024]
    vrow = rows[0:1, 1024:2048]
    lrow = rows[0:1, 2048:3072]
    arow = rows[0:1, 3072:4096]
    brow = rows[0:1, 0:1024]

    pp0 = ExitStack()
    pst = [pp0.enter_context(nc.psum_tensor(f"pst{i}", [128, 512], F32))
           for i in range(4)]

    x_piece = {}
    for pc in range(4):
        x_piece[pc] = bump(cx, 16)
    ev["misc_loaded"] = bump(cx, 64)
    ev["rtr_loaded"] = bump(cw, 16)
    ev["wzn0"] = bump(cw, 16)
    ev["wzn1"] = bump(cw, 16)
    cast_dc = {}
    for dc in range(DC):
        cast_dc[dc] = bump(cdve)
    stats_dc = {}
    for dc in range(DC):
        stats_dc[dc] = bump(cpe)
    ev["var_done"] = bump(cdve)
    ev["rstd_done"] = bump(cact)
    ev["rows_done"] = bump(cdve)
    ev["bcast_done"] = bump(cpe)

    with nc.Block("p0") as blk:
        @blk.sync
        def _(s):
            for pc in range(4):
                s.dma_start(
                    xs[:, pc * 4 * T:(pc + 1) * 4 * T].rearrange(
                        "p (dc t) -> p dc t", t=T),
                    x_d[pc * 512:(pc + 1) * 512, :].rearrange(
                        "(dc p) t -> p dc t", p=128),
                ).then_inc(sm_x, 16)
            for dst, src in ((bz_sb, bz_d), (bn_sb, bn_d), (csel_sb, csel_d),
                             (idf_sb, idf_d)):
                s.dma_start(dst[:, :], src[:, :]).then_inc(sm_x, 16)

        @blk.gpsimd
        def _(g):
            if kperf & 64:
                return
            g.dma_start(rtr_sb[:, :], rtr_d[:, :]).then_inc(sm_w, 16)
            for i in range(2):
                g.dma_start(
                    wzn_buf[i][:, :],
                    wzn_d[i, :, :, :, :].rearrange("p a b c -> p (a b c)"),
                ).then_inc(sm_w, 16)

        @blk.vector
        def _(v):
            v.memset(ones_bf[:, :], 1.0)
            v.memset(onesrow[:, :], 1.0)
            v.memset(epscol[:, :], 1e-5)
            for dc in range(DC):
                v.wait_ge(sm_x, x_piece[dc // 4])
                if dc >= 2:
                    v.wait_ge(sm_pe, stats_dc[dc - 2])
                sl = slice(dc * T, (dc + 1) * T)
                v.tensor_copy(xbr[dc % 2][:, :], xs[:, sl])
                v.tensor_mul(x2r[dc % 2][:, :], xbr[dc % 2][:, :],
                             xbr[dc % 2][:, :]).then_inc(sm_dve, 1)
            v.wait_ge(sm_x, ev["misc_loaded"])
            v.tensor_scalar_mul(negbz[:, :], bz_sb[:, :], -1.0)
            v.tensor_scalar_add(bnp05[:, :], bn_sb[:, :], 0.5)
            v.tensor_copy(identb[:, :], idf_sb[:, :])

        @blk.tensor
        def _(t):
            for dc in range(DC):
                t.wait_ge(sm_dve, cast_dc[dc])
                for si, src in enumerate((xbr, x2r)):
                    for tt in range(NT):
                        last = t.matmul(
                            pst[si * 2 + tt][0:1, :], ones_bf[:, :],
                            src[dc % 2][:, tt * 512:(tt + 1) * 512],
                            start=(dc == 0), stop=(dc == DC - 1),
                            skip_group_check=True)
                last.then_inc(sm_pe, 1)

        @blk.vector
        def _(v):
            v.wait_ge(sm_pe, stats_dc[DC - 1])
            for tt in range(NT):
                tsl = slice(tt * 512, (tt + 1) * 512)
                v.tensor_scalar_mul(mrow[:, tsl], pst[tt][0:1, :], 1.0 / D)
                v.tensor_scalar_mul(vrow[:, tsl], pst[2 + tt][0:1, :], 1.0 / D)
            v.tensor_mul(lrow[:, :], mrow[:, :], mrow[:, :])
            v.tensor_sub(vrow[:, :], vrow[:, :],
                         lrow[:, :]).then_inc(sm_dve, 1)

        @blk.scalar
        def _(sc):
            sc.wait_ge(sm_dve, ev["var_done"])
            sc.activation(lrow[:, :], vrow[:, :], AF.Ln, bias=epscol[:, :])
            sc.activation(arow[:, :], lrow[:, :], AF.Exp,
                          scale=-0.5).then_inc(sm_act, 1)

        @blk.vector
        def _(v):
            v.wait_ge(sm_act, ev["rstd_done"])
            v.tensor_mul(brow[:, :], brow[:, :], arow[:, :])
            v.tensor_scalar_mul(brow[:, :], brow[:, :], -1.0).then_inc(sm_dve, 1)

        @blk.tensor
        def _(t):
            t.wait_ge(sm_dve, ev["rows_done"])
            for ri, row in enumerate((arow, brow)):
                for tt in range(NT):
                    last = t.matmul(pst[ri * 2 + tt][:, :], onesrow[:, :],
                                    row[0:1, tt * 512:(tt + 1) * 512],
                                    start=True, stop=True)
            last.then_inc(sm_pe, 1)

        @blk.vector
        def _(v):
            v.wait_ge(sm_pe, ev["bcast_done"])
            for tt in range(NT):
                v.tensor_copy(a_bc[:, tt * 512:(tt + 1) * 512], pst[tt][:, :])
                v.tensor_copy(b_bc[:, tt * 512:(tt + 1) * 512], pst[2 + tt][:, :])
            for dc in range(DC):
                sl = slice(dc * T, (dc + 1) * T)
                v.tensor_mul(tmpT[:, :], xs[:, sl], a_bc[:, :])
                v.tensor_tensor(xn[:, sl], tmpT[:, :], b_bc[:, :], ALU.add)

    pp0.close()
    p0.close()
    if phases <= 0:
        r1.close()
        ctx.close()
        return nc

    p13 = ExitStack()
    A_store = p13.enter_context(nc.sbuf_tensor("A_store", [128, DC * T], F32))
    b_store = p13.enter_context(nc.sbuf_tensor("b_store", [128, DC * T], F32))

    p1 = ExitStack()
    s1f = lambda name, shape, dt: p1.enter_context(nc.sbuf_tensor(name, shape, dt))
    cbuf = s1f("cbuf", [128, T], F32)
    zbuf = s1f("zbuf", [128, T], F32)
    s1buf = s1f("s1buf", [128, T], F32)
    s2buf = s1f("s2buf", [128, T], F32)
    zeros_t = s1f("zeros_t", [128, T], F32)

    pp1 = ExitStack()
    pg = [[pp1.enter_context(nc.psum_tensor(f"p1_{g}_{i}", [128, 512], F32))
           for i in range(4)] for g in range(2)]

    kn_done = {}
    pe_chunk_done = {}
    act_chunk_done = {}
    dve_chunk_done = {}
    wzn_loaded = {0: ev["wzn0"], 1: ev["wzn1"]}
    for ec in range(DC):
        for part in range(2):
            for tt in range(NT):
                kn_done[(ec, part, tt)] = bump(cpe)
        pe_chunk_done[ec] = cpe[0]
    for ec in range(DC):
        bump(cact, 3)
        act_chunk_done[ec] = cact[0]
    for ec in range(DC):
        dve_chunk_done[ec] = bump(cdve)
    for ec in range(2, DC):
        wzn_loaded[ec] = bump(cw, 16)

    with nc.Block("p1") as blk:
        @blk.vector
        def _(v):
            v.memset(zeros_t[:, :], 0.0)

        @blk.gpsimd
        def _(g):
            if kperf & 64:
                return
            for ec in range(2, DC):
                g.wait_ge(sm_pe, pe_chunk_done[ec - 2])
                g.dma_start(
                    wzn_buf[ec % 2][:, :],
                    wzn_d[ec, :, :, :, :].rearrange("p a b c -> p (a b c)"),
                ).then_inc(sm_w, 16)

        @blk.tensor
        def _(t):
            for ec in range(DC):
                if not (kperf & 64):
                    t.wait_ge(sm_w, wzn_loaded[ec])
                if ec >= 2:
                    t.wait_ge(sm_act, act_chunk_done[ec - 2])
                    t.wait_ge(sm_dve, dve_chunk_done[ec - 2])
                buf = wzn_buf[ec % 2]
                grp = pg[ec % 2]
                for part in range(2):
                    for tt in range(NT):
                        for dc in range(DC):
                            last = t.matmul(
                                grp[part * 2 + tt][:, :],
                                buf[:, part * 2048 + dc * 128:
                                    part * 2048 + (dc + 1) * 128],
                                xn[:, dc * T + tt * 512: dc * T + (tt + 1) * 512],
                                start=(dc == 0), stop=(dc == DC - 1))
                        last.then_inc(sm_pe, 1)

        @blk.scalar
        def _(sc):
            for ec in range(DC):
                grp = pg[ec % 2]
                if ec >= 1:
                    sc.wait_ge(sm_dve, dve_chunk_done[ec - 1])
                for tt in range(NT):
                    sc.wait_ge(sm_pe, kn_done[(ec, 0, tt)])
                    last = sc.activation(cbuf[:, tt * 512:(tt + 1) * 512],
                                         grp[tt][:, :], AF.Sigmoid,
                                         bias=negbz[:, ec:ec + 1], scale=-1.0)
                last.then_inc(sm_act, 1)
                for tt in range(NT):
                    last = sc.activation(zbuf[:, tt * 512:(tt + 1) * 512],
                                         grp[tt][:, :], AF.Sigmoid,
                                         bias=bz_sb[:, ec:ec + 1])
                last.then_inc(sm_act, 1)
                for tt in range(NT):
                    sc.wait_ge(sm_pe, kn_done[(ec, 1, tt)])
                    last = sc.activation(s1buf[:, tt * 512:(tt + 1) * 512],
                                         grp[2 + tt][:, :], AF.Sigmoid,
                                         bias=bn_sb[:, ec:ec + 1])
                last.then_inc(sm_act, 1)

        @blk.vector
        def _(v):
            for ec in range(DC):
                v.wait_ge(sm_act, act_chunk_done[ec])
                grp = pg[ec % 2]
                sl = slice(ec * T, (ec + 1) * T)
                for tt in range(NT):
                    v.tensor_scalar_add(s2buf[:, tt * 512:(tt + 1) * 512],
                                        grp[2 + tt][:, :],
                                        bnp05[:, ec:ec + 1])
                v.tensor_max(s2buf[:, :], s2buf[:, :], s1buf[:, :])
                v.tensor_mul(zbuf[:, :], zbuf[:, :], s2buf[:, :])
                v.tensor_tensor_scan(b_store[:, sl], cbuf[:, :], zbuf[:, :],
                                     0.0, ALU.mult, ALU.add)
                v.tensor_tensor_scan(A_store[:, sl], cbuf[:, :], zeros_t[:, :],
                                     1.0, ALU.mult, ALU.add
                                     ).then_inc(sm_sd, 1)
                csd[0] += 1
                v.wait_ge(sm_sd, csd[0])
                v.tensor_copy(s_out[:, ec:ec + 1],
                              A_store[:, (ec + 1) * T - 1:(ec + 1) * T])
                v.tensor_copy(s_out[:, 16 + ec:17 + ec],
                              b_store[:, (ec + 1) * T - 1:(ec + 1) * T]
                              ).then_inc(sm_dve, 1)

    pp1.close()
    p1.close()
    r1.close()
    if phases <= 1:
        p13.close()
        ctx.close()
        return nc

    r2 = ExitStack()
    sbr = lambda name, shape, dt: r2.enter_context(
        nc.sbuf_tensor(name, shape, dt, side="right"))
    gu_buf = [sbr(f"gub{i}", [128, 4096], BF16) for i in range(2)]
    dn_buf = [sbr(f"dnb{i}", [128, 3072], BF16) for i in range(2)]

    def gu_dma(g, e, fc, par):
        return g.dma_start(
            gu_buf[par][:, :],
            gu_d[e, fc, :, :, :, :].rearrange("p a b c -> p (a b c)"))

    def dn_dma(g, e, dcg, par):
        return g.dma_start(
            dn_buf[par][:, :].rearrange("p (dc f) -> p dc f", dc=4),
            down_d[e, dcg * 4:(dcg + 1) * 4, :, :, :].rearrange(
                "dc p fc d -> p dc (fc d)"))

    ev["ccin_done"] = bump(cx, 16)
    ev["gu00"] = bump(cw, 16)
    ev["gu01"] = bump(cw, 16)
    ev["dn00"] = bump(cw, 16)
    ev["dn01"] = bump(cw, 16)
    ev["ag_done"] = bump(cw, 16)
    fix_c = {}
    for ec in range(DC):
        fix_c[ec] = bump(cdve)
    spill_c = {}
    for ec in range(DC):
        spill_c[ec] = bump(cx, 16)
    ev["spill_done"] = cx[0]

    with nc.Block("p23") as blk:
        @blk.sync
        def _(s):
            s.dma_start(cc_in[:, :], s_out[:, :]).then_inc(sm_x, 16)

        @blk.gpsimd
        def _(g):
            if not (kperf & 64):
                gu_dma(g, 0, 0, 0).then_inc(sm_w, 16)
                gu_dma(g, 0, 1, 1).then_inc(sm_w, 16)
                dn_dma(g, 0, 0, 0).then_inc(sm_w, 16)
                dn_dma(g, 0, 1, 1).then_inc(sm_w, 16)
            g.wait_ge(sm_x, ev["ccin_done"])
            if kperf & 128:
                g.nop().then_inc(sm_cc, 1)
            else:
                g.collective_compute(
                    "AllGather", ALU.bypass,
                    replica_groups=[list(range(NCORES))],
                    ins=[cc_in.ap().opt()],
                    outs=[cc_out.ap().opt()],
                ).then_inc(sm_cc, 1)
            g.wait_ge(sm_cc, 1)
            if kperf & 64:
                g.nop().then_inc(sm_w, 16)
            else:
                g.dma_start(
                    ag_sb[:, :].rearrange("p (r c) -> p r c", c=32),
                    cc_out[:, :].rearrange("(r p) c -> p r c", p=128),
                ).then_inc(sm_w, 16)

        @blk.vector
        def _(v):
            def ss(inst):
                inst.then_inc(sm_sd, 1)
                csd[0] += 1
                v.wait_ge(sm_sd, csd[0])
            v.wait_ge(sm_w, ev["ag_done"])
            v.memset(Hcur[:, :], 0.5)
            ss(v.memset(H0[:, :], 0.0))
            for m in range(NCORES):
                ss(v.tensor_scalar_mul(tmp16a[:, :], Hcur[:, :],
                                       csel_sb[:, m:m + 1]))
                ss(v.tensor_add(H0[:, :], H0[:, :], tmp16a[:, :]))
                if m < NCORES - 1:
                    ss(v.tensor_mul(tmp16b[:, :], Hcur[:, :],
                                    ag_sb[:, m * 32:m * 32 + 16]))
                    ss(v.tensor_add(Hcur[:, :], tmp16b[:, :],
                                    ag_sb[:, m * 32 + 16:m * 32 + 32]))
            for ec in range(DC):
                sl = slice(ec * T, (ec + 1) * T)
                v.scalar_tensor_tensor(
                    A_store[:, sl], A_store[:, sl], H0[:, ec:ec + 1],
                    b_store[:, sl], ALU.mult, ALU.add).then_inc(sm_dve, 1)

        @blk.sync
        def _(s):
            for ec in range(DC):
                s.wait_ge(sm_dve, fix_c[ec])
                s.dma_start(
                    h_scr[ec * 128:(ec + 1) * 128, :],
                    A_store[:, ec * T:(ec + 1) * T]).then_inc(sm_x, 16)

    p13.close()
    if phases <= 2:
        r2.close()
        ctx.close()
        return nc

    p45 = ExitStack()
    OUT_acc = p45.enter_context(nc.sbuf_tensor("OUT_acc", [128, DC * T], F32))

    p4 = ExitStack()
    s4 = lambda name, shape, dt: p4.enter_context(nc.sbuf_tensor(name, shape, dt))
    H_sb = [s4(f"H_sb{i}", [128, FC * T], BF16) for i in range(2)]
    cb_sb = [s4(f"cbsb{i}", [128, T], F32) for i in range(2)]
    sgbuf = [s4(f"sgb{i}", [128, 512], F32) for i in range(2)]
    tmp512 = s4("tmp512", [128, 512], F32)
    sc_sb = s4("sc_sb", [16, 1024], F32)

    p5 = ExitStack()
    s5 = lambda name, shape, dt: p5.enter_context(nc.sbuf_tensor(name, shape, dt))
    hring = [s5(f"hring{i}", [128, T], F32) for i in range(2)]
    xring = [s5(f"xring{i}", [128, T], F32) for i in range(2)]
    oring = [s5(f"oring{i}", [128, T], F32) for i in range(2)]
    silb = [s5(f"silb{i}", [128, T], F32) for i in range(2)]

    pp4 = ExitStack()
    Gp = [pp4.enter_context(nc.psum_tensor(f"Gp{i}", [128, 512], F32))
          for i in range(2)]
    Up = [pp4.enter_context(nc.psum_tensor(f"Up{i}", [128, 512], F32))
          for i in range(2)]
    Ob = [pp4.enter_context(nc.psum_tensor(f"Ob{i}", [128, 512], F32))
          for i in range(2)]
    Cp = [pp4.enter_context(nc.psum_tensor(f"Cp{i}", [128, 512], F32))
          for i in range(2)]
    sc_ps = Cp
    tp_ps = Ob[0]

    ev["sc_done"] = bump(cpe, 2)
    ev["sc_evac"] = bump(cact)
    ev["tp_done"] = bump(cpe, 8)
    ev["negm_done"] = bump(cdve)
    ev["exp_done"] = bump(cact)
    ev["router_done"] = bump(cdve)

    def stage2_seq(e):
        return ([e - 1] if e >= 1 else []) + ([E - 1] if e == E - 1 else [])

    cbc_ps_done = {}
    g_done = {}
    u_done = {}
    o_done = {}
    down_consumed = {}
    for e in range(E):
        for tt in range(NT):
            cbc_ps_done[(e, tt)] = bump(cpe)
        for fc in range(FC):
            for tt in range(NT):
                q = e * 12 + fc * 2 + tt
                g_done[q] = bump(cpe)
                u_done[q] = bump(cpe)
        for ep in stage2_seq(e):
            for dc in range(DC):
                for tt in range(NT):
                    r = ep * 32 + dc * 2 + tt
                    o_done[r] = bump(cpe)
                    if dc % 4 == 3 and tt == 1:
                        down_consumed[(ep, dc // 4)] = cpe[0]
    gu_consumed = [u_done[e * 12 + fc * 2 + 1]
                   for e in range(E) for fc in range(FC)]

    cbc_evac_done = {}
    sg_done = {}
    for e in range(E):
        bump(cact, 2)
        cbc_evac_done[e] = cact[0]
        for fc in range(FC):
            for tt in range(NT):
                sg_done[e * 12 + fc * 2 + tt] = bump(cact)

    H_done_q = {}
    H_tt_done = {}
    H_all_done = {}
    oadd_done = {}
    p5_dve = {}
    for e in range(E):
        for fc in range(FC):
            for tt in range(NT):
                H_done_q[e * 12 + fc * 2 + tt] = bump(cdve)
        H_tt_done[(e, 0)] = H_done_q[e * 12 + 10]
        H_tt_done[(e, 1)] = H_done_q[e * 12 + 11]
        H_all_done[e] = H_done_q[e * 12 + 11]
        for ep in stage2_seq(e):
            for dc in range(DC):
                for tt in range(NT):
                    oadd_done[ep * 32 + dc * 2 + tt] = bump(cdve)
                if ep == E - 1:
                    p5_dve[dc] = bump(cdve)

    gu_loaded = {(0, 0): ev["gu00"], (0, 1): ev["gu01"]}
    down_loaded = {(0, 0): ev["dn00"], (0, 1): ev["dn01"]}
    for e in range(E):
        for fc in range(FC):
            if e * 6 + fc >= 2:
                gu_loaded[(e, fc)] = bump(cw, 16)
        for ep in ([e - 1] if e >= 1 else []) + ([E - 1] if e == E - 1 else []):
            for dcg in range(4):
                if ep * 4 + dcg >= 2:
                    down_loaded[(ep, dcg)] = bump(cw, 16)

    p5_loads = {}
    p5_sil = {}
    p5_store = {}
    for dc in range(DC):
        if dc >= 2:
            p5_store[dc - 2] = bump(cx, 16)
        p5_loads[dc] = bump(cx, 32)
    p5_store[DC - 2] = bump(cx, 16)
    p5_store[DC - 1] = bump(cx, 16)
    for dc in range(DC):
        p5_sil[dc] = bump(cact)

    def emit_stage2(t, ep):
        for dc in range(DC):
            if dc % 4 == 0 and not (kperf & 16):
                t.wait_ge(sm_w, down_loaded[(ep, dc // 4)])
            for tt in range(NT):
                if dc == 0 and not (kperf & 8):
                    t.wait_ge(sm_dve, H_tt_done[(ep, tt)])
                r = ep * 32 + dc * 2 + tt
                if r >= 2 and not (kperf & 8):
                    t.wait_ge(sm_dve, oadd_done[r - 2])
                for fc in range(FC):
                    last = t.matmul(
                        Ob[r % 2][:, :],
                        dn_buf[(ep * 4 + dc // 4) % 2][
                            :, (dc % 4) * 768 + fc * 128:
                            (dc % 4) * 768 + (fc + 1) * 128],
                        H_sb[ep % 2][:, fc * T + tt * 512:
                                     fc * T + (tt + 1) * 512],
                        start=(fc == 0), stop=(fc == FC - 1))
                last.then_inc(sm_pe, 1)

    with nc.Block("p4") as blk:
        @blk.gpsimd
        def _(g):
            if kperf & 16:
                pass
            else:
              for e in range(E):
                for fc in range(FC):
                    j = e * 6 + fc
                    if j < 2:
                        continue
                    g.wait_ge(sm_pe, gu_consumed[j - 2])
                    gu_dma(g, e, fc, j % 2).then_inc(sm_w, 16)
                for ep in ([e - 1] if e >= 1 else []) + \
                        ([E - 1] if e == E - 1 else []):
                    for dcg in range(4):
                        jd = ep * 4 + dcg
                        if jd < 2:
                            continue
                        g.wait_ge(sm_pe,
                                  down_consumed[(ep - (1 if dcg < 2 else 0),
                                                 (dcg + 2) % 4)])
                        dn_dma(g, ep, dcg, jd % 2).then_inc(sm_w, 16)

        @blk.sync
        def _(s):
            s.wait_ge(sm_x, ev["spill_done"])
            for dc in range(DC):
                par = dc % 2
                if dc >= 2:
                    s.wait_ge(sm_dve, p5_dve[dc - 2])
                    s.dma_start(y_d[(dc - 2) * 128:(dc - 1) * 128, :],
                                oring[dc % 2][:, :]).then_inc(sm_x, 16)
                s.dma_start(hring[par][:, :],
                            h_scr[dc * 128:(dc + 1) * 128, :]).then_inc(sm_x, 16)
                s.dma_start(xring[par][:, :],
                            x_d[dc * 128:(dc + 1) * 128, :]).then_inc(sm_x, 16)
            for dc in (DC - 2, DC - 1):
                s.wait_ge(sm_dve, p5_dve[dc])
                s.dma_start(y_d[dc * 128:(dc + 1) * 128, :],
                            oring[dc % 2][:, :]).then_inc(sm_x, 16)
            s.wait_ge(sm_x, p5_store[DC - 1])

        @blk.tensor
        def _(t):
            def pwait(sem, val):
                if not (kperf & 8):
                    t.wait_ge(sem, val)
            for tt in range(NT):
                for dc in range(DC):
                    last = t.matmul(
                        sc_ps[tt][0:16, :], rtr_sb[:, dc * 16:(dc + 1) * 16],
                        xn[:, dc * T + tt * 512: dc * T + (tt + 1) * 512],
                        start=(dc == 0), stop=(dc == DC - 1))
                last.then_inc(sm_pe, 1)
            t.wait_ge(sm_act, ev["sc_evac"])
            for tb in range(8):
                t.transpose(tp_ps[:, tb * 16:(tb + 1) * 16],
                            sc_sb[0:16, tb * 128:(tb + 1) * 128],
                            idf_sb[0:16, 0:16]).then_inc(sm_pe, 1)
            for e in range(E):
                if e == 0:
                    t.wait_ge(sm_dve, ev["router_done"])
                else:
                    pwait(sm_act, cbc_evac_done[e - 1])
                for tt in range(NT):
                    for tb4 in range(4):
                        tb = tt * 4 + tb4
                        lhsT = bass.AP(c_tmb[:, :].tensor, tb * 16 + e,
                                       [[128, 128], [0, 128]])
                        last = t.matmul(Cp[tt][:, tb4 * 128:(tb4 + 1) * 128],
                                        lhsT, identb[:, :],
                                        start=True, stop=True)
                    last.then_inc(sm_pe, 1)
                for fc in range(FC):
                    if not (kperf & 16):
                        t.wait_ge(sm_w, gu_loaded[(e, fc)])
                    for tt in range(NT):
                        q = e * 12 + fc * 2 + tt
                        if q >= 2:
                            pwait(sm_dve, H_done_q[q - 2])
                        buf = gu_buf[(e * 6 + fc) % 2]
                        for half, dst in ((0, Gp[q % 2]), (1, Up[q % 2])):
                            for dc in range(DC):
                                last = t.matmul(
                                    dst[:, :],
                                    buf[:, half * 2048 + dc * 128:
                                        half * 2048 + (dc + 1) * 128],
                                    xn[:, dc * T + tt * 512:
                                       dc * T + (tt + 1) * 512],
                                    start=(dc == 0), stop=(dc == DC - 1))
                            last.then_inc(sm_pe, 1)
                for ep in stage2_seq(e):
                    emit_stage2(t, ep)

        @blk.scalar
        def _(sc):
            def act_op(fn):
                if kperf & 2:
                    return sc.nop()
                return fn()
            sc.wait_ge(sm_pe, ev["sc_done"])
            for tt in range(NT):
                last = sc.copy(sc_sb[0:16, tt * 512:(tt + 1) * 512],
                               sc_ps[tt][0:16, :])
            last.then_inc(sm_act, 1)
            sc.wait_ge(sm_dve, ev["negm_done"])
            for tb in range(8):
                last = sc.activation(e_all[:, tb * 16:(tb + 1) * 16],
                                     st_tm[:, tb * 16:(tb + 1) * 16], AF.Exp,
                                     bias=rcols[:, tb:tb + 1])
            last.then_inc(sm_act, 1)
            for e in range(E):
                for tt in range(NT):
                    sc.wait_ge(sm_pe, cbc_ps_done[(e, tt)])
                    if e >= 2 and tt == 0:
                        sc.wait_ge(sm_dve, H_all_done[e - 2])
                    act_op(lambda: sc.copy(
                        cb_sb[e % 2][:, tt * 512:(tt + 1) * 512],
                        Cp[tt][:, :])).then_inc(sm_act, 1)
                for fc in range(FC):
                    for tt in range(NT):
                        idx = e * 12 + fc * 2 + tt
                        sc.wait_ge(sm_pe, g_done[idx])
                        if idx >= 2:
                            sc.wait_ge(sm_dve, H_done_q[idx - 2])
                        act_op(lambda: sc.activation(
                            sgbuf[idx % 2][:, :], Gp[idx % 2][:, :],
                            AF.Silu)).then_inc(sm_act, 1)
            for dc in range(DC):
                par = dc % 2
                sc.wait_ge(sm_dve, oadd_done[(E - 1) * 32 + dc * 2 + 1])
                if dc >= 2:
                    sc.wait_ge(sm_dve, p5_dve[dc - 2])
                sc.activation(silb[par][:, :],
                              OUT_acc[:, dc * T:(dc + 1) * T],
                              AF.Silu).then_inc(sm_act, 1)

        @blk.vector
        def _(v):
            def ss(inst):
                inst.then_inc(sm_sd, 1)
                csd[0] += 1
                v.wait_ge(sm_sd, csd[0])
            v.wait_ge(sm_pe, ev["tp_done"])
            ss(v.tensor_copy(st_tm[:, :], tp_ps[:, 0:128]))
            st3 = st_tm[:, :].rearrange("p (a b) -> p a b", b=16)
            ss(v.reduce_max(rcols[:, 8:16], st3, AX.X))
            ss(v.tensor_scalar_mul(rcols[:, 0:8], rcols[:, 8:16], -1.0))
            v.engine_nop().then_inc(sm_dve, 1)
            v.wait_ge(sm_act, ev["exp_done"])
            e3 = e_all[:, :].rearrange("p (a b) -> p a b", b=16)
            ss(v.reduce_max(rcols[:, 8:16], e3, AX.X))
            for tb in range(8):
                last = v.tensor_single_scalar(cmp1b[:, tb * 16:(tb + 1) * 16],
                                              e_all[:, tb * 16:(tb + 1) * 16],
                                              rcols[:, 8 + tb:9 + tb],
                                              ALU.is_equal)
            ss(last)
            ss(v.scalar_tensor_tensor(maskb[:, :], cmp1b[:, :], 2.0,
                                      e_all[:, :], ALU.mult, ALU.subtract))
            ss(v.tensor_scalar_mul(maskb[:, :], maskb[:, :], -1.0))
            m3 = maskb[:, :].rearrange("p (a b) -> p a b", b=16)
            ss(v.reduce_max(rcols[:, 16:24], m3, AX.X))
            for tb in range(8):
                last = v.tensor_single_scalar(cmp2b[:, tb * 16:(tb + 1) * 16],
                                              maskb[:, tb * 16:(tb + 1) * 16],
                                              rcols[:, 16 + tb:17 + tb],
                                              ALU.is_equal)
            ss(last)
            ss(v.tensor_add(rcols[:, 24:32], rcols[:, 8:16], rcols[:, 16:24]))
            ss(v.reciprocal(rcols[:, 32:40], rcols[:, 24:32]))
            v.tensor_mul(rcols[:, 40:48], rcols[:, 8:16], rcols[:, 32:40])
            ss(v.tensor_mul(rcols[:, 48:56], rcols[:, 16:24], rcols[:, 32:40]))
            for tb in range(8):
                v.tensor_scalar_mul(cmp1b[:, tb * 16:(tb + 1) * 16],
                                    cmp1b[:, tb * 16:(tb + 1) * 16],
                                    rcols[:, 40 + tb:41 + tb])
                last = v.tensor_scalar_mul(cmp2b[:, tb * 16:(tb + 1) * 16],
                                           cmp2b[:, tb * 16:(tb + 1) * 16],
                                           rcols[:, 48 + tb:49 + tb])
            ss(last)
            ss(v.tensor_add(c_tm[:, :], cmp1b[:, :], cmp2b[:, :]))
            v.tensor_copy(c_tmb[:, :], c_tm[:, :]).then_inc(sm_dve, 1)
            for e in range(E):
                for fc in range(FC):
                    for tt in range(NT):
                        idx = e * 12 + fc * 2 + tt
                        if fc == 0 and tt == 0:
                            v.wait_ge(sm_act, cbc_evac_done[e])
                        v.wait_ge(sm_act, sg_done[idx])
                        v.wait_ge(sm_pe, u_done[idx])
                        if kperf & 1:
                            v.engine_nop().then_inc(sm_dve, 1)
                        else:
                            v.tensor_mul(tmp512[:, :], sgbuf[idx % 2][:, :],
                                         Up[idx % 2][:, :])
                            v.tensor_tensor(
                                H_sb[e % 2][:, fc * T + tt * 512:
                                            fc * T + (tt + 1) * 512],
                                tmp512[:, :],
                                cb_sb[e % 2][:, tt * 512:(tt + 1) * 512],
                                ALU.mult).then_inc(sm_dve, 1)
                for ep in stage2_seq(e):
                    for dc in range(DC):
                        for tt in range(NT):
                            r = ep * 32 + dc * 2 + tt
                            v.wait_ge(sm_pe, o_done[r])
                            osl = OUT_acc[:, dc * T + tt * 512:
                                          dc * T + (tt + 1) * 512]
                            if kperf & 4:
                                v.engine_nop().then_inc(sm_dve, 1)
                            elif ep == 0:
                                v.tensor_copy(osl, Ob[r % 2][:, :]
                                              ).then_inc(sm_dve, 1)
                            else:
                                v.tensor_tensor(osl, osl, Ob[r % 2][:, :],
                                                ALU.add).then_inc(sm_dve, 1)
                        if ep == E - 1:
                            par = dc % 2
                            v.wait_ge(sm_x, p5_loads[dc])
                            v.wait_ge(sm_act, p5_sil[dc])
                            if dc >= 2:
                                v.wait_ge(sm_x, p5_store[dc - 2])
                            v.tensor_mul(oring[par][:, :], hring[par][:, :],
                                         silb[par][:, :])
                            v.tensor_tensor(oring[par][:, :], oring[par][:, :],
                                            xring[par][:, :],
                                            ALU.add).then_inc(sm_dve, 1)

    if phases <= 3:
        p5.close()
        pp4.close()
        p4.close()
        r2.close()
        p45.close()
        ctx.close()
        return nc

    if dbg:
        ev["dbg"] = bump(cx, 112)
        with nc.Block("dbg") as blk:
            @blk.vector
            def _(v):
                v.tensor_copy(oring[0][:, :], xn[:, 0:T]).then_inc(sm_dve, 1)
            ev["dbgv"] = bump(cdve)

            @blk.sync
            def _(s):
                s.dma_start(
                    dbg_h[:, :].rearrange("(dc p) t -> p dc t", p=128),
                    h_scr[:, :].rearrange("(dc p) t -> p dc t", p=128),
                ).then_inc(sm_x, 16)
                s.dma_start(
                    dbg_o[:, :].rearrange("(dc p) t -> p dc t", p=128),
                    OUT_acc[:, :].rearrange("p (dc t) -> p dc t", t=T),
                ).then_inc(sm_x, 16)
                s.wait_ge(sm_dve, ev["dbgv"])
                s.dma_start(dbg_xn[0:128, :], oring[0][:, :]).then_inc(sm_x, 16)
                s.dma_start(dbg_r[:, 0:128], st_tm[:, :]).then_inc(sm_x, 16)
                s.dma_start(dbg_r[:, 128:256], e_all[:, :]).then_inc(sm_x, 16)
                s.dma_start(dbg_r[:, 256:384], c_tm[:, :]).then_inc(sm_x, 16)
                s.dma_start(dbg_r[:, 384:448], rcols[:, :]).then_inc(sm_x, 16)
                s.wait_ge(sm_x, ev["dbg"])

    p5.close()
    pp4.close()
    p4.close()
    r2.close()
    p45.close()
    ctx.close()
    return nc


_NC_CACHE = [None]


def prepare_in_maps(L0_attn_x, L0_attn_Wz, L0_attn_bz, L0_attn_Wn, L0_attn_bn,
                    L0_ffn_router_w, L0_ffn_gate_w, L0_ffn_up_w, L0_ffn_down_w):
    x = np.asarray(L0_attn_x, np.float32).reshape(NCORES * T, D)
    Wz = np.asarray(L0_attn_Wz, np.float32)
    Wn = np.asarray(L0_attn_Wn, np.float32)
    bz = np.asarray(L0_attn_bz, np.float32)
    bn = np.asarray(L0_attn_bn, np.float32)
    rt = np.asarray(L0_ffn_router_w, np.float32)
    gw = np.asarray(L0_ffn_gate_w, np.float32)
    uw = np.asarray(L0_ffn_up_w, np.float32)
    dw = np.asarray(L0_ffn_down_w, np.float32)

    x_fm = np.ascontiguousarray(x.T)
    wz_r = Wz.reshape(16, 128, 16, 128).transpose(0, 3, 2, 1)
    wn_r = Wn.reshape(16, 128, 16, 128).transpose(0, 3, 2, 1)
    wzn = np.ascontiguousarray(np.stack([wz_r, wn_r], axis=2))
    g_r = gw.reshape(16, 16, 128, 6, 128).transpose(0, 3, 2, 1, 4)
    u_r = uw.reshape(16, 16, 128, 6, 128).transpose(0, 3, 2, 1, 4)
    gu = np.ascontiguousarray(np.stack([g_r, u_r], axis=3))
    dn = np.ascontiguousarray(
        dw.reshape(16, 6, 128, 16, 128).transpose(0, 3, 2, 1, 4))
    rtr = np.ascontiguousarray(
        rt.reshape(16, 128, 16).transpose(1, 0, 2)).reshape(128, 256)
    bzr = np.ascontiguousarray(bz.reshape(16, 128).T)
    bnr = np.ascontiguousarray(bn.reshape(16, 128).T)
    idf = np.eye(128, dtype=np.float32)

    in_maps = []
    for i in range(NCORES):
        csel = np.zeros((128, NCORES), np.float32)
        csel[:, i] = 1.0
        in_maps.append(dict(
            x=np.ascontiguousarray(x_fm[:, i * T:(i + 1) * T]),
            wzn=wzn, gu=gu, down=dn, rtr=rtr, bz=bzr, bn=bnr,
            csel=csel, idf=idf,
        ))
    return in_maps


def kernel(**inputs):
    in_maps = prepare_in_maps(**inputs)
    if _NC_CACHE[0] is None:
        _NC_CACHE[0] = build_nc()
    nc = _NC_CACHE[0]
    res = run_bass_kernel_spmd(nc, in_maps, core_ids=list(range(NCORES)))
    y_fm = np.concatenate([res.results[i]["y"] for i in range(NCORES)], axis=1)
    return np.ascontiguousarray(y_fm.T).reshape(1, NCORES * T, D)


# version 12
# speedup vs baseline: 1.0180x; 1.0180x over previous
import os
import numpy as np
from contextlib import ExitStack

import concourse.bass as bass
import concourse.mybir as mybir
from concourse.bass_utils import run_bass_kernel_spmd

F32 = mybir.dt.float32
BF16 = mybir.dt.bfloat16
AF = mybir.ActivationFunctionType
ALU = mybir.AluOpType
AX = mybir.AxisListType

NCORES = 8
D = 2048
T = 1024
DC = 16
E = 16
FC = 6
NT = 2


def build_nc(phases=99, kperf=0):
    nc = bass.Bass("TRN2", target_bir_lowering=False,
                   detect_race_conditions=False)

    x_d = nc.dram_tensor("x", [D, T], F32, kind="ExternalInput")
    wzn_d = nc.dram_tensor("wzn", [16, 128, 2, 16, 128], F32, kind="ExternalInput")
    gu_d = nc.dram_tensor("gu", [16, 6, 128, 2, 16, 128], F32, kind="ExternalInput")
    down_d = nc.dram_tensor("down", [16, 16, 128, 6, 128], F32, kind="ExternalInput")
    rtr_d = nc.dram_tensor("rtr", [128, 256], F32, kind="ExternalInput")
    bz_d = nc.dram_tensor("bz", [128, 16], F32, kind="ExternalInput")
    bn_d = nc.dram_tensor("bn", [128, 16], F32, kind="ExternalInput")
    csel_d = nc.dram_tensor("csel", [128, 8], F32, kind="ExternalInput")
    idf_d = nc.dram_tensor("idf", [128, 128], F32, kind="ExternalInput")
    y_d = nc.dram_tensor("y", [D, T], F32, kind="ExternalOutput")

    h_scr = nc.dram_tensor("h_scr", [D, T], F32)
    dbg = os.environ.get("KDBG") == "1"
    if dbg:
        dbg_h = nc.dram_tensor("dbg_h", [D, T], F32, kind="ExternalOutput")
        dbg_o = nc.dram_tensor("dbg_o", [D, T], F32, kind="ExternalOutput")
        dbg_xn = nc.dram_tensor("dbg_xn", [D, T], F32, kind="ExternalOutput")
        dbg_r = nc.dram_tensor("dbg_r", [128, 512], F32, kind="ExternalOutput")
    cc_in = nc.dram_tensor("cc_in", [128, 32], F32)
    cc_out = nc.dram_tensor("cc_out", [NCORES * 128, 32], F32)

    ctx = ExitStack()
    sbl = lambda name, shape, dt: ctx.enter_context(
        nc.sbuf_tensor(name, shape, dt))

    xn = sbl("xn", [128, DC * T], BF16)
    bz_sb = sbl("bz_sb", [128, 16], F32)
    bn_sb = sbl("bn_sb", [128, 16], F32)
    negbz = sbl("negbz", [128, 16], F32)
    bnp05 = sbl("bnp05", [128, 16], F32)
    rtr_sb = sbl("rtr_sb", [128, 256], BF16)
    csel_sb = sbl("csel_sb", [128, 8], F32)
    idf_sb = sbl("idf_sb", [128, 128], F32)
    identb = sbl("identb", [128, 128], BF16)
    ones_bf = sbl("ones_bf", [128, 1], BF16)
    onesrow = sbl("onesrow", [1, 128], F32)
    s_out = sbl("s_out", [128, 32], F32)
    ag_sb = sbl("ag_sb", [128, NCORES * 32], F32)
    Hcur = sbl("Hcur", [128, 16], F32)
    H0 = sbl("H0", [128, 16], F32)
    tmp16a = sbl("tmp16a", [128, 16], F32)
    tmp16b = sbl("tmp16b", [128, 16], F32)
    c_tm = sbl("c_tm", [128, 128], F32)
    c_tmb = sbl("c_tmb", [128, 128], BF16)
    st_tm = sbl("st_tm", [128, 128], F32)
    e_all = sbl("e_all", [128, 128], F32)
    cmp1b = sbl("cmp1b", [128, 128], F32)
    cmp2b = sbl("cmp2b", [128, 128], F32)
    maskb = sbl("maskb", [128, 128], F32)
    rcols = sbl("rcols", [128, 64], F32)

    r1 = ExitStack()
    wzn_buf = [r1.enter_context(
        nc.sbuf_tensor(f"wznb{i}", [128, 4096], BF16, side="right"))
        for i in range(2)]

    sm_x = ctx.enter_context(nc.semaphore("sm_x"))
    sm_w = ctx.enter_context(nc.semaphore("sm_w"))
    sm_pe = ctx.enter_context(nc.semaphore("sm_pe"))
    sm_act = ctx.enter_context(nc.semaphore("sm_act"))
    sm_dve = ctx.enter_context(nc.semaphore("sm_dve"))
    sm_cc = ctx.enter_context(nc.semaphore("sm_cc"))
    sm_sd = ctx.enter_context(nc.semaphore("sm_sd"))
    csd = [0]

    cx = [0]; cw = [0]; cpe = [0]; cact = [0]; cdve = [0]

    def bump(c, n=1):
        c[0] += n
        return c[0]

    ev = {}

    p0 = ExitStack()
    s0 = lambda name, shape, dt: p0.enter_context(nc.sbuf_tensor(name, shape, dt))
    xs = s0("xs", [128, DC * T], F32)
    xbr = [s0(f"xbr{i}", [128, T], BF16) for i in range(2)]
    x2r = [s0(f"x2r{i}", [128, T], BF16) for i in range(2)]
    a_bc = s0("a_bc", [128, T], F32)
    b_bc = s0("b_bc", [128, T], F32)
    tmpT = s0("tmpT", [128, T], F32)
    rows = s0("rows", [1, 4096], F32)
    epscol = s0("epscol", [1, 1], F32)
    mrow = rows[0:1, 0:1024]
    vrow = rows[0:1, 1024:2048]
    lrow = rows[0:1, 2048:3072]
    arow = rows[0:1, 3072:4096]
    brow = rows[0:1, 0:1024]

    pp0 = ExitStack()
    pst = [pp0.enter_context(nc.psum_tensor(f"pst{i}", [128, 512], F32))
           for i in range(4)]

    x_piece = {}
    for pc in range(4):
        x_piece[pc] = bump(cx, 16)
    ev["misc_loaded"] = bump(cx, 64)
    ev["rtr_loaded"] = bump(cw, 16)
    ev["wzn0"] = bump(cw, 16)
    ev["wzn1"] = bump(cw, 16)
    cast_dc = {}
    for dc in range(DC):
        cast_dc[dc] = bump(cdve)
    stats_dc = {}
    for dc in range(DC):
        stats_dc[dc] = bump(cpe)
    ev["var_done"] = bump(cdve)
    ev["rstd_done"] = bump(cact)
    ev["rows_done"] = bump(cdve)
    ev["bcast_done"] = bump(cpe)

    with nc.Block("p0") as blk:
        @blk.sync
        def _(s):
            for pc in range(4):
                s.dma_start(
                    xs[:, pc * 4 * T:(pc + 1) * 4 * T].rearrange(
                        "p (dc t) -> p dc t", t=T),
                    x_d[pc * 512:(pc + 1) * 512, :].rearrange(
                        "(dc p) t -> p dc t", p=128),
                ).then_inc(sm_x, 16)
            for dst, src in ((bz_sb, bz_d), (bn_sb, bn_d), (csel_sb, csel_d),
                             (idf_sb, idf_d)):
                s.dma_start(dst[:, :], src[:, :]).then_inc(sm_x, 16)

        @blk.gpsimd
        def _(g):
            if kperf & 64:
                return
            g.dma_start(rtr_sb[:, :], rtr_d[:, :]).then_inc(sm_w, 16)
            for i in range(2):
                g.dma_start(
                    wzn_buf[i][:, :],
                    wzn_d[i, :, :, :, :].rearrange("p a b c -> p (a b c)"),
                ).then_inc(sm_w, 16)

        @blk.vector
        def _(v):
            v.memset(ones_bf[:, :], 1.0)
            v.memset(onesrow[:, :], 1.0)
            v.memset(epscol[:, :], 1e-5)
            for dc in range(DC):
                v.wait_ge(sm_x, x_piece[dc // 4])
                if dc >= 2:
                    v.wait_ge(sm_pe, stats_dc[dc - 2])
                sl = slice(dc * T, (dc + 1) * T)
                v.tensor_copy(xbr[dc % 2][:, :], xs[:, sl])
                v.tensor_mul(x2r[dc % 2][:, :], xbr[dc % 2][:, :],
                             xbr[dc % 2][:, :]).then_inc(sm_dve, 1)
            v.wait_ge(sm_x, ev["misc_loaded"])
            v.tensor_scalar_mul(negbz[:, :], bz_sb[:, :], -1.0)
            v.tensor_scalar_add(bnp05[:, :], bn_sb[:, :], 0.5)
            v.tensor_copy(identb[:, :], idf_sb[:, :])

        @blk.tensor
        def _(t):
            for dc in range(DC):
                t.wait_ge(sm_dve, cast_dc[dc])
                for si, src in enumerate((xbr, x2r)):
                    for tt in range(NT):
                        last = t.matmul(
                            pst[si * 2 + tt][0:1, :], ones_bf[:, :],
                            src[dc % 2][:, tt * 512:(tt + 1) * 512],
                            start=(dc == 0), stop=(dc == DC - 1),
                            skip_group_check=True)
                last.then_inc(sm_pe, 1)

        @blk.vector
        def _(v):
            v.wait_ge(sm_pe, stats_dc[DC - 1])
            for tt in range(NT):
                tsl = slice(tt * 512, (tt + 1) * 512)
                v.tensor_scalar_mul(mrow[:, tsl], pst[tt][0:1, :], 1.0 / D)
                v.tensor_scalar_mul(vrow[:, tsl], pst[2 + tt][0:1, :], 1.0 / D)
            v.tensor_mul(lrow[:, :], mrow[:, :], mrow[:, :])
            v.tensor_sub(vrow[:, :], vrow[:, :],
                         lrow[:, :]).then_inc(sm_dve, 1)

        @blk.scalar
        def _(sc):
            sc.wait_ge(sm_dve, ev["var_done"])
            sc.activation(lrow[:, :], vrow[:, :], AF.Ln, bias=epscol[:, :])
            sc.activation(arow[:, :], lrow[:, :], AF.Exp,
                          scale=-0.5).then_inc(sm_act, 1)

        @blk.vector
        def _(v):
            v.wait_ge(sm_act, ev["rstd_done"])
            v.tensor_mul(brow[:, :], brow[:, :], arow[:, :])
            v.tensor_scalar_mul(brow[:, :], brow[:, :], -1.0).then_inc(sm_dve, 1)

        @blk.tensor
        def _(t):
            t.wait_ge(sm_dve, ev["rows_done"])
            for ri, row in enumerate((arow, brow)):
                for tt in range(NT):
                    last = t.matmul(pst[ri * 2 + tt][:, :], onesrow[:, :],
                                    row[0:1, tt * 512:(tt + 1) * 512],
                                    start=True, stop=True)
            last.then_inc(sm_pe, 1)

        @blk.vector
        def _(v):
            v.wait_ge(sm_pe, ev["bcast_done"])
            for tt in range(NT):
                v.tensor_copy(a_bc[:, tt * 512:(tt + 1) * 512], pst[tt][:, :])
                v.tensor_copy(b_bc[:, tt * 512:(tt + 1) * 512], pst[2 + tt][:, :])
            for dc in range(DC):
                sl = slice(dc * T, (dc + 1) * T)
                v.tensor_mul(tmpT[:, :], xs[:, sl], a_bc[:, :])
                v.tensor_tensor(xn[:, sl], tmpT[:, :], b_bc[:, :], ALU.add)

    pp0.close()
    p0.close()
    if phases <= 0:
        r1.close()
        ctx.close()
        return nc

    p13 = ExitStack()
    A_store = p13.enter_context(nc.sbuf_tensor("A_store", [128, DC * T], F32))
    b_store = p13.enter_context(nc.sbuf_tensor("b_store", [128, DC * T], F32))

    p1 = ExitStack()
    s1f = lambda name, shape, dt: p1.enter_context(nc.sbuf_tensor(name, shape, dt))
    cbuf = s1f("cbuf", [128, T], F32)
    zbuf = s1f("zbuf", [128, T], F32)
    s1buf = s1f("s1buf", [128, T], F32)
    s2buf = s1f("s2buf", [128, T], F32)
    zeros_t = s1f("zeros_t", [128, T], F32)

    pp1 = ExitStack()
    pg = [[pp1.enter_context(nc.psum_tensor(f"p1_{g}_{i}", [128, 512], F32))
           for i in range(4)] for g in range(2)]

    kn_done = {}
    pe_chunk_done = {}
    act_chunk_done = {}
    dve_chunk_done = {}
    wzn_loaded = {0: ev["wzn0"], 1: ev["wzn1"]}
    for ec in range(DC):
        for part in range(2):
            for tt in range(NT):
                kn_done[(ec, part, tt)] = bump(cpe)
        pe_chunk_done[ec] = cpe[0]
    for ec in range(DC):
        bump(cact, 3)
        act_chunk_done[ec] = cact[0]
    for ec in range(DC):
        dve_chunk_done[ec] = bump(cdve)
    for ec in range(2, DC):
        wzn_loaded[ec] = bump(cw, 16)

    with nc.Block("p1") as blk:
        @blk.vector
        def _(v):
            v.memset(zeros_t[:, :], 0.0)

        @blk.gpsimd
        def _(g):
            if kperf & 64:
                return
            for ec in range(2, DC):
                g.wait_ge(sm_pe, pe_chunk_done[ec - 2])
                g.dma_start(
                    wzn_buf[ec % 2][:, :],
                    wzn_d[ec, :, :, :, :].rearrange("p a b c -> p (a b c)"),
                ).then_inc(sm_w, 16)

        @blk.tensor
        def _(t):
            for ec in range(DC):
                if not (kperf & 64):
                    t.wait_ge(sm_w, wzn_loaded[ec])
                if ec >= 2:
                    t.wait_ge(sm_act, act_chunk_done[ec - 2])
                    t.wait_ge(sm_dve, dve_chunk_done[ec - 2])
                buf = wzn_buf[ec % 2]
                grp = pg[ec % 2]
                for part in range(2):
                    for tt in range(NT):
                        for dc in range(DC):
                            last = t.matmul(
                                grp[part * 2 + tt][:, :],
                                buf[:, part * 2048 + dc * 128:
                                    part * 2048 + (dc + 1) * 128],
                                xn[:, dc * T + tt * 512: dc * T + (tt + 1) * 512],
                                start=(dc == 0), stop=(dc == DC - 1))
                        last.then_inc(sm_pe, 1)

        @blk.scalar
        def _(sc):
            for ec in range(DC):
                grp = pg[ec % 2]
                if ec >= 1:
                    sc.wait_ge(sm_dve, dve_chunk_done[ec - 1])
                for tt in range(NT):
                    sc.wait_ge(sm_pe, kn_done[(ec, 0, tt)])
                    last = sc.activation(cbuf[:, tt * 512:(tt + 1) * 512],
                                         grp[tt][:, :], AF.Sigmoid,
                                         bias=negbz[:, ec:ec + 1], scale=-1.0)
                last.then_inc(sm_act, 1)
                for tt in range(NT):
                    last = sc.activation(zbuf[:, tt * 512:(tt + 1) * 512],
                                         grp[tt][:, :], AF.Sigmoid,
                                         bias=bz_sb[:, ec:ec + 1])
                last.then_inc(sm_act, 1)
                for tt in range(NT):
                    sc.wait_ge(sm_pe, kn_done[(ec, 1, tt)])
                    last = sc.activation(s1buf[:, tt * 512:(tt + 1) * 512],
                                         grp[2 + tt][:, :], AF.Sigmoid,
                                         bias=bn_sb[:, ec:ec + 1])
                last.then_inc(sm_act, 1)

        @blk.vector
        def _(v):
            for ec in range(DC):
                v.wait_ge(sm_act, act_chunk_done[ec])
                grp = pg[ec % 2]
                sl = slice(ec * T, (ec + 1) * T)
                for tt in range(NT):
                    v.tensor_scalar_add(s2buf[:, tt * 512:(tt + 1) * 512],
                                        grp[2 + tt][:, :],
                                        bnp05[:, ec:ec + 1])
                v.tensor_max(s2buf[:, :], s2buf[:, :], s1buf[:, :])
                v.tensor_mul(zbuf[:, :], zbuf[:, :], s2buf[:, :])
                v.tensor_tensor_scan(b_store[:, sl], cbuf[:, :], zbuf[:, :],
                                     0.0, ALU.mult, ALU.add)
                v.tensor_tensor_scan(A_store[:, sl], cbuf[:, :], zeros_t[:, :],
                                     1.0, ALU.mult, ALU.add
                                     ).then_inc(sm_sd, 1)
                csd[0] += 1
                v.wait_ge(sm_sd, csd[0])
                v.tensor_copy(s_out[:, ec:ec + 1],
                              A_store[:, (ec + 1) * T - 1:(ec + 1) * T])
                v.tensor_copy(s_out[:, 16 + ec:17 + ec],
                              b_store[:, (ec + 1) * T - 1:(ec + 1) * T]
                              ).then_inc(sm_dve, 1)

    pp1.close()
    p1.close()
    r1.close()
    if phases <= 1:
        p13.close()
        ctx.close()
        return nc

    r2 = ExitStack()
    sbr = lambda name, shape, dt: r2.enter_context(
        nc.sbuf_tensor(name, shape, dt, side="right"))
    gu_buf = [sbr(f"gub{i}", [128, 4096], BF16) for i in range(2)]
    dn_buf = [sbr(f"dnb{i}", [128, 3072], BF16) for i in range(2)]

    def gu_dma(g, e, fc, par):
        return g.dma_start(
            gu_buf[par][:, :],
            gu_d[e, fc, :, :, :, :].rearrange("p a b c -> p (a b c)"))

    def dn_dma(g, e, dcg, par):
        return g.dma_start(
            dn_buf[par][:, :].rearrange("p (dc f) -> p dc f", dc=4),
            down_d[e, dcg * 4:(dcg + 1) * 4, :, :, :].rearrange(
                "dc p fc d -> p dc (fc d)"))

    ev["ccin_done"] = bump(cx, 16)
    ev["ag_done"] = bump(cx, 16)
    ev["gu00"] = bump(cw, 16)
    ev["gu01"] = bump(cw, 16)
    ev["dn00"] = bump(cw, 16)
    ev["dn01"] = bump(cw, 16)
    fix_c = {}
    for ec in range(DC):
        fix_c[ec] = bump(cdve)
    spill_c = {}
    for ec in range(DC):
        spill_c[ec] = bump(cx, 16)
    ev["spill_done"] = cx[0]

    p13.close()
    if phases <= 2:
        r2.close()
        ctx.close()
        return nc

    p45 = ExitStack()
    OUT_acc = p45.enter_context(nc.sbuf_tensor("OUT_acc", [128, DC * T], F32))

    p4 = ExitStack()
    s4 = lambda name, shape, dt: p4.enter_context(nc.sbuf_tensor(name, shape, dt))
    H_sb = [s4(f"H_sb{i}", [128, FC * T], BF16) for i in range(2)]
    cb_sb = [s4(f"cbsb{i}", [128, T], F32) for i in range(2)]
    sgbuf = [s4(f"sgb{i}", [128, 512], F32) for i in range(2)]
    tmp512 = s4("tmp512", [128, 512], F32)
    sc_sb = s4("sc_sb", [16, 1024], F32)

    p5 = ExitStack()
    s5 = lambda name, shape, dt: p5.enter_context(nc.sbuf_tensor(name, shape, dt))
    hring = [s5(f"hring{i}", [128, T], F32) for i in range(2)]
    xring = [s5(f"xring{i}", [128, T], F32) for i in range(2)]
    oring = [s5(f"oring{i}", [128, T], F32) for i in range(2)]
    silb = [s5(f"silb{i}", [128, T], F32) for i in range(2)]

    pp4 = ExitStack()
    Gp = [pp4.enter_context(nc.psum_tensor(f"Gp{i}", [128, 512], F32))
          for i in range(2)]
    Up = [pp4.enter_context(nc.psum_tensor(f"Up{i}", [128, 512], F32))
          for i in range(2)]
    Ob = [pp4.enter_context(nc.psum_tensor(f"Ob{i}", [128, 512], F32))
          for i in range(2)]
    Cp = [pp4.enter_context(nc.psum_tensor(f"Cp{i}", [128, 512], F32))
          for i in range(2)]
    sc_ps = Cp
    tp_ps = Ob[0]

    ev["sc_done"] = bump(cpe, 2)
    ev["sc_evac"] = bump(cact)
    ev["tp_done"] = bump(cpe, 8)
    ev["negm_done"] = bump(cdve)
    ev["exp_done"] = bump(cact)
    ev["router_done"] = bump(cdve)

    def stage2_seq(e):
        return ([e - 1] if e >= 1 else []) + ([E - 1] if e == E - 1 else [])

    cbc_ps_done = {}
    g_done = {}
    u_done = {}
    o_done = {}
    down_consumed = {}
    for e in range(E):
        for tt in range(NT):
            cbc_ps_done[(e, tt)] = bump(cpe)
        for fc in range(FC):
            for tt in range(NT):
                q = e * 12 + fc * 2 + tt
                g_done[q] = bump(cpe)
                u_done[q] = bump(cpe)
        for ep in stage2_seq(e):
            for dc in range(DC):
                for tt in range(NT):
                    r = ep * 32 + dc * 2 + tt
                    o_done[r] = bump(cpe)
                    if dc % 4 == 3 and tt == 1:
                        down_consumed[(ep, dc // 4)] = cpe[0]
    gu_consumed = [u_done[e * 12 + fc * 2 + 1]
                   for e in range(E) for fc in range(FC)]

    cbc_evac_done = {}
    sg_done = {}
    for e in range(E):
        bump(cact, 2)
        cbc_evac_done[e] = cact[0]
        for fc in range(FC):
            for tt in range(NT):
                sg_done[e * 12 + fc * 2 + tt] = bump(cact)

    H_done_q = {}
    H_tt_done = {}
    H_all_done = {}
    oadd_done = {}
    p5_dve = {}
    for e in range(E):
        for fc in range(FC):
            for tt in range(NT):
                H_done_q[e * 12 + fc * 2 + tt] = bump(cdve)
        H_tt_done[(e, 0)] = H_done_q[e * 12 + 10]
        H_tt_done[(e, 1)] = H_done_q[e * 12 + 11]
        H_all_done[e] = H_done_q[e * 12 + 11]
        for ep in stage2_seq(e):
            for dc in range(DC):
                for tt in range(NT):
                    oadd_done[ep * 32 + dc * 2 + tt] = bump(cdve)
                if ep == E - 1:
                    p5_dve[dc] = bump(cdve)

    gu_loaded = {(0, 0): ev["gu00"], (0, 1): ev["gu01"]}
    down_loaded = {(0, 0): ev["dn00"], (0, 1): ev["dn01"]}
    for e in range(E):
        for fc in range(FC):
            if e * 6 + fc >= 2:
                gu_loaded[(e, fc)] = bump(cw, 16)
        for ep in ([e - 1] if e >= 1 else []) + ([E - 1] if e == E - 1 else []):
            for dcg in range(4):
                if ep * 4 + dcg >= 2:
                    down_loaded[(ep, dcg)] = bump(cw, 16)

    p5_loads = {}
    p5_sil = {}
    p5_store = {}
    for dc in range(DC):
        if dc >= 2:
            p5_store[dc - 2] = bump(cx, 16)
        p5_loads[dc] = bump(cx, 32)
    p5_store[DC - 2] = bump(cx, 16)
    p5_store[DC - 1] = bump(cx, 16)
    for dc in range(DC):
        p5_sil[dc] = bump(cact)

    def emit_stage2(t, ep):
        for dc in range(DC):
            if dc % 4 == 0 and not (kperf & 16):
                t.wait_ge(sm_w, down_loaded[(ep, dc // 4)])
            for tt in range(NT):
                if dc == 0 and not (kperf & 8):
                    t.wait_ge(sm_dve, H_tt_done[(ep, tt)])
                r = ep * 32 + dc * 2 + tt
                if r >= 2 and not (kperf & 8):
                    t.wait_ge(sm_dve, oadd_done[r - 2])
                for fc in range(FC):
                    last = t.matmul(
                        Ob[r % 2][:, :],
                        dn_buf[(ep * 4 + dc // 4) % 2][
                            :, (dc % 4) * 768 + fc * 128:
                            (dc % 4) * 768 + (fc + 1) * 128],
                        H_sb[ep % 2][:, fc * T + tt * 512:
                                     fc * T + (tt + 1) * 512],
                        start=(fc == 0), stop=(fc == FC - 1))
                last.then_inc(sm_pe, 1)

    with nc.Block("p4") as blk:
        @blk.sync
        def _(s):
            s.dma_start(cc_in[:, :], s_out[:, :]).then_inc(sm_x, 16)
            s.wait_ge(sm_cc, 1)
            s.dma_start(
                ag_sb[:, :].rearrange("p (r c) -> p r c", c=32),
                cc_out[:, :].rearrange("(r p) c -> p r c", p=128),
            ).then_inc(sm_x, 16)
            for ec in range(DC):
                s.wait_ge(sm_dve, fix_c[ec])
                s.dma_start(
                    h_scr[ec * 128:(ec + 1) * 128, :],
                    A_store[:, ec * T:(ec + 1) * T]).then_inc(sm_x, 16)

        @blk.vector
        def _(v):
            def ss(inst):
                inst.then_inc(sm_sd, 1)
                csd[0] += 1
                v.wait_ge(sm_sd, csd[0])
            v.wait_ge(sm_x, ev["ag_done"])
            v.memset(Hcur[:, :], 0.5)
            ss(v.memset(H0[:, :], 0.0))
            for m in range(NCORES):
                ss(v.tensor_scalar_mul(tmp16a[:, :], Hcur[:, :],
                                       csel_sb[:, m:m + 1]))
                ss(v.tensor_add(H0[:, :], H0[:, :], tmp16a[:, :]))
                if m < NCORES - 1:
                    ss(v.tensor_mul(tmp16b[:, :], Hcur[:, :],
                                    ag_sb[:, m * 32:m * 32 + 16]))
                    ss(v.tensor_add(Hcur[:, :], tmp16b[:, :],
                                    ag_sb[:, m * 32 + 16:m * 32 + 32]))
            for ec in range(DC):
                sl = slice(ec * T, (ec + 1) * T)
                v.scalar_tensor_tensor(
                    A_store[:, sl], A_store[:, sl], H0[:, ec:ec + 1],
                    b_store[:, sl], ALU.mult, ALU.add).then_inc(sm_dve, 1)


        @blk.gpsimd
        def _(g):
            gu_dma(g, 0, 0, 0).then_inc(sm_w, 16)
            gu_dma(g, 0, 1, 1).then_inc(sm_w, 16)
            dn_dma(g, 0, 0, 0).then_inc(sm_w, 16)
            dn_dma(g, 0, 1, 1).then_inc(sm_w, 16)
            g.wait_ge(sm_x, ev["ccin_done"])
            g.collective_compute(
                "AllGather", ALU.bypass,
                replica_groups=[list(range(NCORES))],
                ins=[cc_in.ap().opt()],
                outs=[cc_out.ap().opt()],
            ).then_inc(sm_cc, 1)

        @blk.gpsimd
        def _(g):
            if kperf & 16:
                pass
            else:
              for e in range(E):
                for fc in range(FC):
                    j = e * 6 + fc
                    if j < 2:
                        continue
                    g.wait_ge(sm_pe, gu_consumed[j - 2])
                    gu_dma(g, e, fc, j % 2).then_inc(sm_w, 16)
                for ep in ([e - 1] if e >= 1 else []) + \
                        ([E - 1] if e == E - 1 else []):
                    for dcg in range(4):
                        jd = ep * 4 + dcg
                        if jd < 2:
                            continue
                        g.wait_ge(sm_pe,
                                  down_consumed[(ep - (1 if dcg < 2 else 0),
                                                 (dcg + 2) % 4)])
                        dn_dma(g, ep, dcg, jd % 2).then_inc(sm_w, 16)

        @blk.sync
        def _(s):
            s.wait_ge(sm_x, ev["spill_done"])
            for dc in range(DC):
                par = dc % 2
                if dc >= 2:
                    s.wait_ge(sm_dve, p5_dve[dc - 2])
                    s.dma_start(y_d[(dc - 2) * 128:(dc - 1) * 128, :],
                                oring[dc % 2][:, :]).then_inc(sm_x, 16)
                s.dma_start(hring[par][:, :],
                            h_scr[dc * 128:(dc + 1) * 128, :]).then_inc(sm_x, 16)
                s.dma_start(xring[par][:, :],
                            x_d[dc * 128:(dc + 1) * 128, :]).then_inc(sm_x, 16)
            for dc in (DC - 2, DC - 1):
                s.wait_ge(sm_dve, p5_dve[dc])
                s.dma_start(y_d[dc * 128:(dc + 1) * 128, :],
                            oring[dc % 2][:, :]).then_inc(sm_x, 16)
            s.wait_ge(sm_x, p5_store[DC - 1])

        @blk.tensor
        def _(t):
            def pwait(sem, val):
                if not (kperf & 8):
                    t.wait_ge(sem, val)
            for tt in range(NT):
                for dc in range(DC):
                    last = t.matmul(
                        sc_ps[tt][0:16, :], rtr_sb[:, dc * 16:(dc + 1) * 16],
                        xn[:, dc * T + tt * 512: dc * T + (tt + 1) * 512],
                        start=(dc == 0), stop=(dc == DC - 1))
                last.then_inc(sm_pe, 1)
            t.wait_ge(sm_act, ev["sc_evac"])
            for tb in range(8):
                t.transpose(tp_ps[:, tb * 16:(tb + 1) * 16],
                            sc_sb[0:16, tb * 128:(tb + 1) * 128],
                            idf_sb[0:16, 0:16]).then_inc(sm_pe, 1)
            for e in range(E):
                if e == 0:
                    t.wait_ge(sm_dve, ev["router_done"])
                else:
                    pwait(sm_act, cbc_evac_done[e - 1])
                for tt in range(NT):
                    for tb4 in range(4):
                        tb = tt * 4 + tb4
                        lhsT = bass.AP(c_tmb[:, :].tensor, tb * 16 + e,
                                       [[128, 128], [0, 128]])
                        last = t.matmul(Cp[tt][:, tb4 * 128:(tb4 + 1) * 128],
                                        lhsT, identb[:, :],
                                        start=True, stop=True)
                    last.then_inc(sm_pe, 1)
                for fc in range(FC):
                    if not (kperf & 16):
                        t.wait_ge(sm_w, gu_loaded[(e, fc)])
                    for tt in range(NT):
                        q = e * 12 + fc * 2 + tt
                        if q >= 2:
                            pwait(sm_dve, H_done_q[q - 2])
                        buf = gu_buf[(e * 6 + fc) % 2]
                        for half, dst in ((0, Gp[q % 2]), (1, Up[q % 2])):
                            for dc in range(DC):
                                last = t.matmul(
                                    dst[:, :],
                                    buf[:, half * 2048 + dc * 128:
                                        half * 2048 + (dc + 1) * 128],
                                    xn[:, dc * T + tt * 512:
                                       dc * T + (tt + 1) * 512],
                                    start=(dc == 0), stop=(dc == DC - 1))
                            last.then_inc(sm_pe, 1)
                for ep in stage2_seq(e):
                    emit_stage2(t, ep)

        @blk.scalar
        def _(sc):
            def act_op(fn):
                if kperf & 2:
                    return sc.nop()
                return fn()
            sc.wait_ge(sm_dve, fix_c[DC - 1])
            sc.wait_ge(sm_pe, ev["sc_done"])
            for tt in range(NT):
                last = sc.copy(sc_sb[0:16, tt * 512:(tt + 1) * 512],
                               sc_ps[tt][0:16, :])
            last.then_inc(sm_act, 1)
            sc.wait_ge(sm_dve, ev["negm_done"])
            for tb in range(8):
                last = sc.activation(e_all[:, tb * 16:(tb + 1) * 16],
                                     st_tm[:, tb * 16:(tb + 1) * 16], AF.Exp,
                                     bias=rcols[:, tb:tb + 1])
            last.then_inc(sm_act, 1)
            for e in range(E):
                for tt in range(NT):
                    sc.wait_ge(sm_pe, cbc_ps_done[(e, tt)])
                    if e >= 2 and tt == 0:
                        sc.wait_ge(sm_dve, H_all_done[e - 2])
                    act_op(lambda: sc.copy(
                        cb_sb[e % 2][:, tt * 512:(tt + 1) * 512],
                        Cp[tt][:, :])).then_inc(sm_act, 1)
                for fc in range(FC):
                    for tt in range(NT):
                        idx = e * 12 + fc * 2 + tt
                        sc.wait_ge(sm_pe, g_done[idx])
                        if idx >= 2:
                            sc.wait_ge(sm_dve, H_done_q[idx - 2])
                        act_op(lambda: sc.activation(
                            sgbuf[idx % 2][:, :], Gp[idx % 2][:, :],
                            AF.Silu)).then_inc(sm_act, 1)
            for dc in range(DC):
                par = dc % 2
                sc.wait_ge(sm_dve, oadd_done[(E - 1) * 32 + dc * 2 + 1])
                if dc >= 2:
                    sc.wait_ge(sm_dve, p5_dve[dc - 2])
                sc.activation(silb[par][:, :],
                              OUT_acc[:, dc * T:(dc + 1) * T],
                              AF.Silu).then_inc(sm_act, 1)

        @blk.vector
        def _(v):
            def ss(inst):
                inst.then_inc(sm_sd, 1)
                csd[0] += 1
                v.wait_ge(sm_sd, csd[0])
            v.wait_ge(sm_pe, ev["tp_done"])
            ss(v.tensor_copy(st_tm[:, :], tp_ps[:, 0:128]))
            st3 = st_tm[:, :].rearrange("p (a b) -> p a b", b=16)
            ss(v.reduce_max(rcols[:, 8:16], st3, AX.X))
            ss(v.tensor_scalar_mul(rcols[:, 0:8], rcols[:, 8:16], -1.0))
            v.engine_nop().then_inc(sm_dve, 1)
            v.wait_ge(sm_act, ev["exp_done"])
            e3 = e_all[:, :].rearrange("p (a b) -> p a b", b=16)
            ss(v.reduce_max(rcols[:, 8:16], e3, AX.X))
            for tb in range(8):
                last = v.tensor_single_scalar(cmp1b[:, tb * 16:(tb + 1) * 16],
                                              e_all[:, tb * 16:(tb + 1) * 16],
                                              rcols[:, 8 + tb:9 + tb],
                                              ALU.is_equal)
            ss(last)
            ss(v.scalar_tensor_tensor(maskb[:, :], cmp1b[:, :], 2.0,
                                      e_all[:, :], ALU.mult, ALU.subtract))
            ss(v.tensor_scalar_mul(maskb[:, :], maskb[:, :], -1.0))
            m3 = maskb[:, :].rearrange("p (a b) -> p a b", b=16)
            ss(v.reduce_max(rcols[:, 16:24], m3, AX.X))
            for tb in range(8):
                last = v.tensor_single_scalar(cmp2b[:, tb * 16:(tb + 1) * 16],
                                              maskb[:, tb * 16:(tb + 1) * 16],
                                              rcols[:, 16 + tb:17 + tb],
                                              ALU.is_equal)
            ss(last)
            ss(v.tensor_add(rcols[:, 24:32], rcols[:, 8:16], rcols[:, 16:24]))
            ss(v.reciprocal(rcols[:, 32:40], rcols[:, 24:32]))
            v.tensor_mul(rcols[:, 40:48], rcols[:, 8:16], rcols[:, 32:40])
            ss(v.tensor_mul(rcols[:, 48:56], rcols[:, 16:24], rcols[:, 32:40]))
            for tb in range(8):
                v.tensor_scalar_mul(cmp1b[:, tb * 16:(tb + 1) * 16],
                                    cmp1b[:, tb * 16:(tb + 1) * 16],
                                    rcols[:, 40 + tb:41 + tb])
                last = v.tensor_scalar_mul(cmp2b[:, tb * 16:(tb + 1) * 16],
                                           cmp2b[:, tb * 16:(tb + 1) * 16],
                                           rcols[:, 48 + tb:49 + tb])
            ss(last)
            ss(v.tensor_add(c_tm[:, :], cmp1b[:, :], cmp2b[:, :]))
            v.tensor_copy(c_tmb[:, :], c_tm[:, :]).then_inc(sm_dve, 1)
            for e in range(E):
                for fc in range(FC):
                    for tt in range(NT):
                        idx = e * 12 + fc * 2 + tt
                        if fc == 0 and tt == 0:
                            v.wait_ge(sm_act, cbc_evac_done[e])
                        v.wait_ge(sm_act, sg_done[idx])
                        v.wait_ge(sm_pe, u_done[idx])
                        if kperf & 1:
                            v.engine_nop().then_inc(sm_dve, 1)
                        else:
                            v.tensor_mul(tmp512[:, :], sgbuf[idx % 2][:, :],
                                         Up[idx % 2][:, :])
                            v.tensor_tensor(
                                H_sb[e % 2][:, fc * T + tt * 512:
                                            fc * T + (tt + 1) * 512],
                                tmp512[:, :],
                                cb_sb[e % 2][:, tt * 512:(tt + 1) * 512],
                                ALU.mult).then_inc(sm_dve, 1)
                for ep in stage2_seq(e):
                    for dc in range(DC):
                        for tt in range(NT):
                            r = ep * 32 + dc * 2 + tt
                            if r == 0:
                                v.wait_ge(sm_x, ev["spill_done"])
                            v.wait_ge(sm_pe, o_done[r])
                            osl = OUT_acc[:, dc * T + tt * 512:
                                          dc * T + (tt + 1) * 512]
                            if kperf & 4:
                                v.engine_nop().then_inc(sm_dve, 1)
                            elif ep == 0:
                                v.tensor_copy(osl, Ob[r % 2][:, :]
                                              ).then_inc(sm_dve, 1)
                            else:
                                v.tensor_tensor(osl, osl, Ob[r % 2][:, :],
                                                ALU.add).then_inc(sm_dve, 1)
                        if ep == E - 1:
                            par = dc % 2
                            v.wait_ge(sm_x, p5_loads[dc])
                            v.wait_ge(sm_act, p5_sil[dc])
                            if dc >= 2:
                                v.wait_ge(sm_x, p5_store[dc - 2])
                            v.tensor_mul(oring[par][:, :], hring[par][:, :],
                                         silb[par][:, :])
                            v.tensor_tensor(oring[par][:, :], oring[par][:, :],
                                            xring[par][:, :],
                                            ALU.add).then_inc(sm_dve, 1)

    if phases <= 3:
        p5.close()
        pp4.close()
        p4.close()
        r2.close()
        p45.close()
        ctx.close()
        return nc

    if dbg:
        ev["dbg"] = bump(cx, 112)
        with nc.Block("dbg") as blk:
            @blk.vector
            def _(v):
                v.tensor_copy(oring[0][:, :], xn[:, 0:T]).then_inc(sm_dve, 1)
            ev["dbgv"] = bump(cdve)

            @blk.sync
            def _(s):
                s.dma_start(
                    dbg_h[:, :].rearrange("(dc p) t -> p dc t", p=128),
                    h_scr[:, :].rearrange("(dc p) t -> p dc t", p=128),
                ).then_inc(sm_x, 16)
                s.dma_start(
                    dbg_o[:, :].rearrange("(dc p) t -> p dc t", p=128),
                    OUT_acc[:, :].rearrange("p (dc t) -> p dc t", t=T),
                ).then_inc(sm_x, 16)
                s.wait_ge(sm_dve, ev["dbgv"])
                s.dma_start(dbg_xn[0:128, :], oring[0][:, :]).then_inc(sm_x, 16)
                s.dma_start(dbg_r[:, 0:128], st_tm[:, :]).then_inc(sm_x, 16)
                s.dma_start(dbg_r[:, 128:256], e_all[:, :]).then_inc(sm_x, 16)
                s.dma_start(dbg_r[:, 256:384], c_tm[:, :]).then_inc(sm_x, 16)
                s.dma_start(dbg_r[:, 384:448], rcols[:, :]).then_inc(sm_x, 16)
                s.wait_ge(sm_x, ev["dbg"])

    p5.close()
    pp4.close()
    p4.close()
    r2.close()
    p45.close()
    ctx.close()
    return nc


_NC_CACHE = [None]


def prepare_in_maps(L0_attn_x, L0_attn_Wz, L0_attn_bz, L0_attn_Wn, L0_attn_bn,
                    L0_ffn_router_w, L0_ffn_gate_w, L0_ffn_up_w, L0_ffn_down_w):
    x = np.asarray(L0_attn_x, np.float32).reshape(NCORES * T, D)
    Wz = np.asarray(L0_attn_Wz, np.float32)
    Wn = np.asarray(L0_attn_Wn, np.float32)
    bz = np.asarray(L0_attn_bz, np.float32)
    bn = np.asarray(L0_attn_bn, np.float32)
    rt = np.asarray(L0_ffn_router_w, np.float32)
    gw = np.asarray(L0_ffn_gate_w, np.float32)
    uw = np.asarray(L0_ffn_up_w, np.float32)
    dw = np.asarray(L0_ffn_down_w, np.float32)

    x_fm = np.ascontiguousarray(x.T)
    wz_r = Wz.reshape(16, 128, 16, 128).transpose(0, 3, 2, 1)
    wn_r = Wn.reshape(16, 128, 16, 128).transpose(0, 3, 2, 1)
    wzn = np.ascontiguousarray(np.stack([wz_r, wn_r], axis=2))
    g_r = gw.reshape(16, 16, 128, 6, 128).transpose(0, 3, 2, 1, 4)
    u_r = uw.reshape(16, 16, 128, 6, 128).transpose(0, 3, 2, 1, 4)
    gu = np.ascontiguousarray(np.stack([g_r, u_r], axis=3))
    dn = np.ascontiguousarray(
        dw.reshape(16, 6, 128, 16, 128).transpose(0, 3, 2, 1, 4))
    rtr = np.ascontiguousarray(
        rt.reshape(16, 128, 16).transpose(1, 0, 2)).reshape(128, 256)
    bzr = np.ascontiguousarray(bz.reshape(16, 128).T)
    bnr = np.ascontiguousarray(bn.reshape(16, 128).T)
    idf = np.eye(128, dtype=np.float32)

    in_maps = []
    for i in range(NCORES):
        csel = np.zeros((128, NCORES), np.float32)
        csel[:, i] = 1.0
        in_maps.append(dict(
            x=np.ascontiguousarray(x_fm[:, i * T:(i + 1) * T]),
            wzn=wzn, gu=gu, down=dn, rtr=rtr, bz=bzr, bn=bnr,
            csel=csel, idf=idf,
        ))
    return in_maps


def kernel(**inputs):
    in_maps = prepare_in_maps(**inputs)
    if _NC_CACHE[0] is None:
        _NC_CACHE[0] = build_nc()
    nc = _NC_CACHE[0]
    res = run_bass_kernel_spmd(nc, in_maps, core_ids=list(range(NCORES)))
    y_fm = np.concatenate([res.results[i]["y"] for i in range(NCORES)], axis=1)
    return np.ascontiguousarray(y_fm.T).reshape(1, NCORES * T, D)
